# version 12
# baseline (speedup 1.0000x reference)
import numpy as np
import ml_dtypes
from contextlib import ExitStack
import concourse.bass as bass
import concourse.mybir as mybir
from concourse.bass_utils import run_bass_kernel_spmd

F32 = mybir.dt.float32
BF16 = mybir.dt.bfloat16
I32 = mybir.dt.int32
AF = mybir.ActivationFunctionType
ALU = mybir.AluOpType
AX = mybir.AxisListType

D = 2048
HCN = 16
TQ = 1024
NTT = 8
NQ = 4
NH = 16
HD = 128
F = 1408
NFC = 11
NE = 8
FS = 2816
EPS = 1e-5
SCL = 128.0 ** -0.5
TCORE = 4096
LA = 2


class Ctr:
    def __init__(self, sem):
        self.sem = sem
        self.val = 0

    def inc(self, inst, n=1):
        inst.then_inc(self.sem, n)
        self.val += n
        return self.val


def build_program(nq=NQ):
    nc = bass.Bass("TRN2", target_bir_lowering=False)

    x_in = nc.dram_tensor("x", [TCORE, D], F32, kind="ExternalInput").ap()
    wguT = nc.dram_tensor("wguT", [D, NE, 2 * F], F32, kind="ExternalInput").ap()
    wdnT = nc.dram_tensor("wdnT", [NE, F, D], F32, kind="ExternalInput").ap()
    sguT = nc.dram_tensor("sguT", [D, 2 * FS], F32, kind="ExternalInput").ap()
    sdnT = nc.dram_tensor("sdnT", [FS, D], F32, kind="ExternalInput").ap()
    gateT = nc.dram_tensor("gateT", [D, NE], F32, kind="ExternalInput").ap()
    ident_h = nc.dram_tensor("ident128", [128, 128], BF16, kind="ExternalInput").ap()
    out = nc.dram_tensor("out", [TCORE, D], F32, kind="ExternalOutput").ap()

    with ExitStack() as ctx:
        En = ctx.enter_context

        ybuf = En(nc.sbuf_tensor("ybuf", [128, NTT * D], F32))
        x1b = En(nc.sbuf_tensor("x1b", [128, NTT * D], BF16))
        xnt = En(nc.sbuf_tensor("xnt", [128, NTT * D], BF16))
        xnT = En(nc.sbuf_tensor("xnT", [128, HCN * TQ], BF16))
        wgu = En(nc.sbuf_tensor("wgu", [128, 2 * HCN * 256], BF16))
        wdn = En(nc.sbuf_tensor("wdn", [128, 2 * NFC * 512], BF16))
        gts = En(nc.sbuf_tensor("gts", [128, HCN * 8], BF16))
        ident = En(nc.sbuf_tensor("ident", [128, 128], BF16))
        bnst = En(nc.sbuf_tensor("bnst", [128, NTT * 24], F32))
        mv = En(nc.sbuf_tensor("mv", [128, NTT * 2], F32))
        rstd = En(nc.sbuf_tensor("rstd", [128, NTT], F32))
        nmr = En(nc.sbuf_tensor("nmr", [128, NTT], F32))
        lgs = En(nc.sbuf_tensor("lgs", [128, 64], F32))
        gsc = En(nc.sbuf_tensor("gsc", [128, 64], F32))
        selm = En(nc.sbuf_tensor("selm", [128, 64], F32))
        comb = En(nc.sbuf_tensor("comb", [128, 64], F32))
        gst = En(nc.sbuf_tensor("gst", [128, 24], F32))
        smx = En(nc.sbuf_tensor("smx", [128, 24], F32))
        expb = En(nc.sbuf_tensor("expb", [128, 3 * 512], BF16))
        attT = En(nc.sbuf_tensor("attT", [128, 3 * 512], BF16))

        pb = [En(nc.psum_tensor(f"pb{i}", [128, 512], F32)) for i in range(6)]
        tb = [En(nc.psum_tensor(f"tb{i}", [128, 1024], BF16)) for i in range(2)]

        tn = Ctr(En(nc.semaphore("tn")))
        vc = Ctr(En(nc.semaphore("vc")))
        sc = Ctr(En(nc.semaphore("sc")))
        gp = Ctr(En(nc.semaphore("gp")))
        d_wgu = [Ctr(En(nc.semaphore("d_wgu0"))), Ctr(En(nc.semaphore("d_wgu1")))]
        d_wdn = [Ctr(En(nc.semaphore("d_wdn0"))), Ctr(En(nc.semaphore("d_wdn1")))]
        d_xin = Ctr(En(nc.semaphore("d_xin")))
        d_xt = [Ctr(En(nc.semaphore(f"d_xt{i}"))) for i in range(NTT)]
        d_gt = Ctr(En(nc.semaphore("d_gt")))
        d_out = Ctr(En(nc.semaphore("d_out")))

        from contextlib import contextmanager
        main_block = En(nc.Block(no_gpsimd_drain=True))

        @contextmanager
        def BLK():
            yield main_block

        ex_list = ["sA", "sB"] + list(range(NE))

        def gu_srcs(ex, fc):
            if ex == "sA":
                t, og, ou = sguT.tensor, fc * 128, 2 * F + fc * 128
                dims = [[2 * FS, 128], [128 * 2 * FS, HCN], [1, 128]]
            elif ex == "sB":
                t, og, ou = sguT.tensor, F + fc * 128, 2 * F + F + fc * 128
                dims = [[2 * FS, 128], [128 * 2 * FS, HCN], [1, 128]]
            else:
                t = wguT.tensor
                og = ex * 2 * F + fc * 128
                ou = ex * 2 * F + F + fc * 128
                dims = [[NE * 2 * F, 128], [128 * NE * 2 * F, HCN], [1, 128]]
            return (bass.AP(t, og, [list(d) for d in dims]),
                    bass.AP(t, ou, [list(d) for d in dims]))

        def dn_src(ex, h4):
            h0 = h4 * 512
            if ex == "sA":
                return bass.AP(sdnT.tensor, h0, [[D, 128], [128 * D, NFC], [1, 512]])
            if ex == "sB":
                return bass.AP(sdnT.tensor, F * D + h0,
                               [[D, 128], [128 * D, NFC], [1, 512]])
            return bass.AP(wdnT.tensor, ex * F * D + h0,
                           [[D, 128], [128 * D, NFC], [1, 512]])

        gu_q = [(ex, fc) for _ in range(nq) for ex in ex_list for fc in range(NFC)]
        dn_q = [(ex, h4) for _ in range(nq) for ex in ex_list for h4 in range(4)]
        gu_done = {}
        dn_done = {}
        gu_issued = [0]
        dn_issued = [0]

        def issue_gu_upto(g, j):
            while gu_issued[0] <= min(j, len(gu_q) - 1):
                k = gu_issued[0]
                if k - 2 >= 0:
                    g.wait_ge(tn.sem, gu_done[k - 2])
                ga, ua = gu_srcs(*gu_q[k])
                par = (k % 2) * HCN * 256
                dg = bass.AP(wgu, par, [[2 * HCN * 256, 128], [256, HCN], [1, 128]])
                du = bass.AP(wgu, par + 128, [[2 * HCN * 256, 128], [256, HCN], [1, 128]])
                d_wgu[k % 2].inc(g.dma_start(dg, ga), 16)
                d_wgu[k % 2].inc(g.dma_start(du, ua), 16)
                gu_issued[0] += 1

        def issue_dn_upto(g, j):
            while dn_issued[0] <= min(j, len(dn_q) - 1):
                k = dn_issued[0]
                if k - 2 >= 0:
                    g.wait_ge(tn.sem, dn_done[k - 2])
                sa = dn_src(*dn_q[k])
                par = (k % 2) * NFC * 512
                dd = bass.AP(wdn, par, [[2 * NFC * 512, 128], [512, NFC], [1, 512]])
                d_wdn[k % 2].inc(g.dma_start(dd, sa), 16)
                dn_issued[0] += 1

        bank_guard = [None] * 6
        tb_guard = [None] * 2

        def bwait(t, i):
            if bank_guard[i] is not None:
                t.wait_ge(bank_guard[i][0], bank_guard[i][1])

        def tbwait(t, i):
            if tb_guard[i] is not None:
                t.wait_ge(tb_guard[i][0], tb_guard[i][1])

        with BLK() as b:
            @b.gpsimd
            def _(g):
                src = bass.AP(gateT.tensor, 0, [[NE, 128], [128 * NE, HCN], [1, NE]])
                d_gt.inc(g.dma_start(gts[:, :], src), 16)
                d_gt.inc(g.dma_start(ident[:, :], ident_h[:, :]), 16)
                for t in range(NTT):
                    srcx = bass.AP(x_in.tensor, t * 128 * D, [[D, 128], [1, D]])
                    d_xin.inc(g.dma_start(x1b[:, t * D:(t + 1) * D], srcx), 16)
                issue_gu_upto(g, LA - 1)
                issue_dn_upto(g, 0)

        def xt(buf, t):
            return buf[:, t * D:(t + 1) * D]

        def ln_site(src_buf, dst_buf, dst_dtype_note, wait_before_stats,
                    wait_before_apply, post_apply_inc, src_bf=False,
                    per_tile_stat_wait=None):
            with BLK() as b:
                @b.vector
                def _(v):
                    wait_before_stats(v)
                    for t in range(NTT):
                        if per_tile_stat_wait is not None:
                            per_tile_stat_wait(v, t)
                        for cch in range(4):
                            v.bn_stats(bnst[:, t * 24 + cch * 6: t * 24 + cch * 6 + 6],
                                       src_buf[:, t * D + cch * 512: t * D + (cch + 1) * 512])
                    vc.inc(v.drain())
                    v.wait_ge(vc.sem, vc.val)
                    for t in range(NTT):
                        v.bn_aggr(mv[:, 2 * t:2 * t + 2], bnst[:, t * 24:(t + 1) * 24])
                    vc.inc(v.drain())
                    v.wait_ge(vc.sem, vc.val)
                    var_ap = bass.AP(mv, 1, [[2 * NTT, 128], [2, NTT]])
                    vc.inc(v.tensor_scalar(rstd[:, :], var_ap, EPS, None, op0=ALU.add))
                    v.wait_ge(vc.sem, vc.val)
                    vc.inc(v.reciprocal(rstd[:, :], rstd[:, :]))
                v_rec = vc.val

                @b.scalar
                def _(s):
                    s.wait_ge(vc.sem, v_rec)
                    sc.inc(s.activation(rstd[:, :], rstd[:, :], AF.Sqrt))
                    s.wait_ge(vc.sem, v_rec + 2)
                    wait_before_apply(s)
                    for t in range(NTT):
                        ins = s.activation(xt(dst_buf, t), xt(src_buf, t), AF.Identity,
                                           bias=nmr[:, t:t + 1], scale=rstd[:, t:t + 1])
                        sc.inc(ins)
                s_sqrt = sc.val - NTT

                @b.vector
                def _(v):
                    v.wait_ge(sc.sem, s_sqrt)
                    mean_ap = bass.AP(mv, 0, [[2 * NTT, 128], [2, NTT]])
                    vc.inc(v.tensor_tensor(nmr[:, :], mean_ap, rstd[:, :], op=ALU.mult))
                    v.wait_ge(vc.sem, vc.val)
                    vc.inc(v.tensor_scalar(nmr[:, :], nmr[:, :], -1.0, None, op0=ALU.mult))
                assert vc.val == v_rec + 2
            return sc.val

        def emit_xn_transposes(apply_base):
            with BLK() as b:
                ev_base = vc.val

                @b.tensor
                def _(t, apply_base=apply_base, ev_base=ev_base):
                    t.wait_ge(d_gt.sem, 32)
                    for tt in range(NTT):
                        t.wait_ge(sc.sem, apply_base + tt + 1)
                        for half in range(2):
                            tbwait(t, half)
                            for hcl in range(8):
                                hc = half * 8 + hcl
                                ins = t.transpose(
                                    tb[half][:, hcl * 128:(hcl + 1) * 128],
                                    xnt[:, tt * D + hc * 128: tt * D + hc * 128 + 128],
                                    ident[:, :])
                            tn.inc(ins)
                            tb_guard[half] = (vc.sem, ev_base + 2 * tt + half + 1)

                @b.vector
                def _(v, ev_base=ev_base):
                    base = tn.val - 2 * NTT
                    for tt in range(NTT):
                        for half in range(2):
                            v.wait_ge(tn.sem, base + 2 * tt + half + 1)
                            for hcl in range(8):
                                hc = half * 8 + hcl
                                ins = v.tensor_copy(
                                    xnT[:, hc * TQ + tt * 128: hc * TQ + (tt + 1) * 128],
                                    tb[half][:, hcl * 128:(hcl + 1) * 128])
                            vc.inc(ins)
            assert vc.val == ev_base + 2 * NTT
            return ev_base

        fin1_done = 0
        prev_out_dma = 0
        prev_down_tn = 0
        prev_q_attn_usersc = 0

        for q in range(nq):
            if q > 0:
                with BLK() as b:
                    @b.gpsimd
                    def _(g):
                        for t in range(NTT):
                            g.wait_ge(sc.sem, fin1_done - NTT + t + 1)
                            srcx = bass.AP(x_in.tensor, (q * TQ + t * 128) * D,
                                           [[D, 128], [1, D]])
                            d_xt[t].inc(g.dma_start(x1b[:, t * D:(t + 1) * D], srcx), 16)
            xt_done = [c.val for c in d_xt]

            def w_stats1(v, _q=q):
                if _q == 0:
                    v.wait_ge(d_xin.sem, d_xin.val)
                else:
                    v.wait_ge(tn.sem, prev_down_tn)

            def ptw1(v, t, _q=q, _xt=xt_done):
                if _q > 0:
                    v.wait_ge(d_xt[t].sem, _xt[t])

            def w_apply1(s, _q=q):
                if _q > 0:
                    s.wait_ge(sc.sem, fin1_done)
            ln1_sc = ln_site(x1b, xnt, None, w_stats1, w_apply1, None, src_bf=True,
                             per_tile_stat_wait=ptw1)

            with BLK() as b:
                @b.vector
                def _(v):
                    if q > 0:
                        v.wait_ge(d_out.sem, prev_out_dma)
                    vc.inc(v.memset(ybuf[:, :], 0.0))
            memset_done = vc.val

            ev1_base = emit_xn_transposes(ln1_sc - NTT)
            xnT1_vc = vc.val

            smax_slot = 0
            attn_sbase = sc.val
            sc_jobs = []
            for ls in range(2):
                for hc in range(NH):
                    par = hc % 2
                    g_off = ls * 512 + par * 256
                    tA = 4 * ls + 2 * par
                    it = ls * NH + hc
                    tbi = it % 2
                    ebuf = it % 3
                    qk_bank_i = (0, 1, 4)[it % 3]
                    av_bank_i = (2, 3, 5)[it % 3]
                    qkb = pb[qk_bank_i]
                    avb = pb[av_bank_i]
                    exp_sc = attn_sbase + 2 * it + 1
                    avev_sc = attn_sbase + 2 * it + 2
                    with BLK() as b:
                        @b.tensor
                        def _(t, hc=hc, ls=ls, g_off=g_off, qk_bank_i=qk_bank_i,
                              qkb=qkb, ev1_base=ev1_base):
                            t.wait_ge(vc.sem,
                                      ev1_base + 2 * (4 * ls + 3) + (hc // 8) + 1)
                            bwait(t, qk_bank_i)
                            g_ap = xnT[:, hc * TQ + g_off: hc * TQ + g_off + 256]
                            for qc in range(2):
                                ins = t.matmul(qkb[:, qc * 256:(qc + 1) * 256],
                                               xnT[:, hc * TQ + g_off + qc * 128:
                                                   hc * TQ + g_off + qc * 128 + 128],
                                               g_ap, start=True, stop=True)
                            tn.inc(ins)
                        qk_tn = tn.val

                        @b.vector
                        def _(v, qkb=qkb, slot=smax_slot, qk_tn=qk_tn):
                            v.wait_ge(tn.sem, qk_tn)
                            v.reduce_max(smx[:, slot: slot + 2],
                                         bass.AP(qkb, 0, [[512, 128], [256, 2], [1, 256]]),
                                         axis=AX.X, negate=True).then_inc(vc.sem, 1)
                            vc.val += 1
                            v.wait_ge(vc.sem, vc.val)
                            vc.inc(v.tensor_scalar(smx[:, slot + 2: slot + 4],
                                                   smx[:, slot: slot + 2],
                                                   SCL, None, op0=ALU.mult))
                        nm_vc = vc.val

                        sc_jobs.append(("exp", qkb, smax_slot, ebuf, nm_vc))
                        bank_guard[qk_bank_i] = (sc.sem, exp_sc)

                        @b.vector
                        def _(v, slot=smax_slot, ebuf=ebuf, exp_sc=exp_sc):
                            v.wait_ge(sc.sem, exp_sc)
                            v.reduce_sum(smx[:, slot + 4: slot + 6],
                                         bass.AP(expb, ebuf * 512,
                                                 [[3 * 512, 128], [256, 2], [1, 256]]),
                                         axis=AX.X).then_inc(vc.sem, 1)
                            vc.val += 1
                            v.wait_ge(vc.sem, vc.val)
                            vc.inc(v.reciprocal(smx[:, slot + 6: slot + 8],
                                                smx[:, slot + 4: slot + 6]))
                        rinv_vc = vc.val

                        @b.tensor
                        def _(t, ebuf=ebuf, tbi=tbi, exp_sc=exp_sc):
                            t.wait_ge(sc.sem, exp_sc)
                            tbwait(t, tbi)
                            for kc in range(2):
                                for qc in range(2):
                                    ins = t.transpose(
                                        tb[tbi][:, kc * 256 + qc * 128:
                                                kc * 256 + qc * 128 + 128],
                                        expb[:, ebuf * 512 + qc * 256 + kc * 128:
                                             ebuf * 512 + qc * 256 + kc * 128 + 128],
                                        ident[:, :])
                            tn.inc(ins)
                        trp_tn = tn.val
                        tb_guard[tbi] = (vc.sem, vc.val + 1)

                        @b.vector
                        def _(v, ebuf=ebuf, tbi=tbi, trp_tn=trp_tn):
                            v.wait_ge(tn.sem, trp_tn)
                            vc.inc(v.tensor_copy(attT[:, ebuf * 512:(ebuf + 1) * 512],
                                                 tb[tbi][:, 0:512]))
                        att_vc = vc.val

                        @b.tensor
                        def _(t, hc=hc, tA=tA, ebuf=ebuf, av_bank_i=av_bank_i,
                              avb=avb, att_vc=att_vc):
                            t.wait_ge(vc.sem, att_vc)
                            bwait(t, av_bank_i)
                            for qc in range(2):
                                for kc in range(2):
                                    ins = t.matmul(
                                        avb[:, qc * 128:(qc + 1) * 128],
                                        attT[:, ebuf * 512 + kc * 256 + qc * 128:
                                             ebuf * 512 + kc * 256 + qc * 128 + 128],
                                        xnt[:, (tA + kc) * D + hc * 128:
                                            (tA + kc) * D + hc * 128 + 128],
                                        start=(kc == 0), stop=(kc == 1))
                            tn.inc(ins)
                        av_tn = tn.val

                        sc_jobs.append(("avev", avb, smax_slot, (tA, hc),
                                        (av_tn, rinv_vc, memset_done)))
                        bank_guard[av_bank_i] = (sc.sem, avev_sc)
                    smax_slot = (smax_slot + 8) % 24
            attn_vc = vc.val

            with BLK() as b:
                @b.scalar
                def _(s, jobs=sc_jobs):
                    for kind, bankt, slot, aux, waits in [
                            (j[0], j[1], j[2], j[3], j[4]) for j in jobs]:
                        if kind == "exp":
                            ebuf, nm_vc = aux, waits
                            s.wait_ge(vc.sem, nm_vc)
                            for qc in range(2):
                                ins = s.activation(
                                    expb[:, ebuf * 512 + qc * 256:
                                         ebuf * 512 + (qc + 1) * 256],
                                    bankt[:, qc * 256:(qc + 1) * 256], AF.Exp,
                                    bias=smx[:, slot + 2 + qc: slot + 3 + qc],
                                    scale=SCL)
                            sc.inc(ins)
                        else:
                            tA, hc = aux
                            av_tn, rinv_vc, msd = waits
                            s.wait_ge(tn.sem, av_tn)
                            s.wait_ge(vc.sem, rinv_vc)
                            s.wait_ge(vc.sem, msd)
                            for qc in range(2):
                                ins = s.activation(
                                    ybuf[:, (tA + qc) * D + hc * 128:
                                         (tA + qc) * D + hc * 128 + 128],
                                    bankt[:, qc * 128:(qc + 1) * 128], AF.Identity,
                                    scale=smx[:, slot + 6 + qc: slot + 7 + qc])
                            sc.inc(ins)
            assert sc.val == attn_sbase + 4 * NH
            attn_done_sc = sc.val

            def w_stats2(v, _m=memset_done):
                v.wait_ge(vc.sem, _m)

            def ptw2(v, t, _b=attn_sbase):
                ls = t // 4
                par = (t // 2) % 2
                it = ls * NH + (14 + par)
                v.wait_ge(sc.sem, _b + 2 * it + 2)

            def w_apply2(s):
                pass
            ln2_sc = ln_site(ybuf, xnt, None, w_stats2, w_apply2, None,
                             per_tile_stat_wait=ptw2)
            res1_base = vc.val
            with BLK() as b:
                @b.vector
                def _(v):
                    for t in range(NTT):
                        v.wait_ge(sc.sem, ln2_sc - NTT + t + 1)
                        ins = v.tensor_tensor(xt(x1b, t), xt(x1b, t), xt(xnt, t),
                                              op=ALU.add)
                        vc.inc(ins)
            res1_vc = vc.val

            def w_stats3(v):
                pass

            def w_apply3(s):
                pass

            def ptw3(v, t, _b=res1_base):
                v.wait_ge(vc.sem, _b + t + 1)
            ln3_sc = ln_site(x1b, xnt, None, w_stats3, w_apply3, None, src_bf=True,
                             per_tile_stat_wait=ptw3)

            emit_xn_transposes(ln3_sc - NTT)
            xnT2_vc = vc.val

            with BLK() as b:
                @b.tensor
                def _(t):
                    t.wait_ge(vc.sem, xnT2_vc)
                    t.wait_ge(d_gt.sem, 32)
                    bwait(t, 0)
                    for tt in range(NTT):
                        for hc in range(HCN):
                            ins = t.matmul(pb[0][:, tt * 8:(tt + 1) * 8],
                                           xnT[:, hc * TQ + tt * 128: hc * TQ + tt * 128 + 128],
                                           gts[:, hc * 8:(hc + 1) * 8],
                                           start=(hc == 0), stop=(hc == HCN - 1))
                    tn.inc(ins)
                gat_tn = tn.val

                @b.vector
                def _(v):
                    v.wait_ge(tn.sem, gat_tn)
                    vc.inc(v.tensor_copy(lgs[:, :], pb[0][:, 0:64]))
                    bank_guard[0] = (vc.sem, vc.val)
                    v.wait_ge(vc.sem, vc.val)
                    seg = lambda buf, o=0: bass.AP(buf, o, [[64, 128], [8, 8], [1, 8]])
                    vc.inc(v.reduce_max(gst[:, 0:8], seg(lgs), axis=AX.X, negate=True))
                    v.wait_ge(vc.sem, vc.val)
                    nmx1b = bass.AP(gst, 0, [[24, 128], [1, 8], [0, 8]])
                    vc.inc(v.tensor_tensor(seg(gsc), seg(lgs), nmx1b, op=ALU.add))
                    v.wait_ge(vc.sem, vc.val)
                    vc.inc(v.tensor_scalar(selm[:, :], gsc[:, :], 0.0, None, op0=ALU.is_ge))
                    v.wait_ge(vc.sem, vc.val)
                    vc.inc(v.scalar_tensor_tensor(selm[:, :], selm[:, :], -1e9, lgs[:, :],
                                                  op0=ALU.mult, op1=ALU.add))
                    v.wait_ge(vc.sem, vc.val)
                    vc.inc(v.reduce_max(gst[:, 8:16], seg(selm), axis=AX.X, negate=True))
                    v.wait_ge(vc.sem, vc.val)
                    nmx2b = bass.AP(gst, 8, [[24, 128], [1, 8], [0, 8]])
                    vc.inc(v.tensor_tensor(seg(selm), seg(lgs), nmx2b, op=ALU.add))
                    v.wait_ge(vc.sem, vc.val)
                    vc.inc(v.tensor_scalar(selm[:, :], selm[:, :], 0.0, None, op0=ALU.is_ge))
                gsub_vc = vc.val

                @b.scalar
                def _(s):
                    s.wait_ge(vc.sem, gsub_vc)
                    sc.inc(s.activation(comb[:, :], gsc[:, :], AF.Exp))
                gexp_sc = sc.val

                @b.vector
                def _(v):
                    v.wait_ge(sc.sem, gexp_sc)
                    seg = lambda buf, o=0: bass.AP(buf, o, [[64, 128], [8, 8], [1, 8]])
                    vc.inc(v.reduce_sum(gst[:, 16:24], seg(comb), axis=AX.X))
                    v.wait_ge(vc.sem, vc.val)
                    vc.inc(v.reciprocal(gst[:, 16:24], gst[:, 16:24]))
                    v.wait_ge(vc.sem, vc.val)
                    rinvb = bass.AP(gst, 16, [[24, 128], [1, 8], [0, 8]])
                    vc.inc(v.tensor_tensor(seg(comb), seg(comb), rinvb, op=ALU.mult))
                    v.wait_ge(vc.sem, vc.val)
                    vc.inc(v.tensor_tensor(comb[:, :], comb[:, :], selm[:, :], op=ALU.mult))
            comb_vc = vc.val

            q_slab0 = q * len(ex_list) * NFC
            q_dslab0 = q * len(ex_list) * 4
            ex_down_tn_prev = prev_down_tn
            sil_base = sc.val
            silu_jobs = []
            for ei, ex in enumerate(ex_list):
                is_first = (ei == 0)
                for fc in range(NFC):
                    gsl = q_slab0 + ei * NFC + fc
                    grp = (gsl * 2) % 3
                    with BLK() as b:
                        @b.gpsimd
                        def _(g, gsl=gsl, fc=fc, ei=ei):
                            issue_gu_upto(g, gsl + 1)
                            if fc == 0:
                                issue_dn_upto(g, q_dslab0 + ei * 4)

                        @b.tensor
                        def _(t, gsl=gsl, fc=fc, ei=ei,
                              _actwar=ex_down_tn_prev):
                            t.wait_ge(d_wgu[gsl % 2].sem, 32 * (gsl // 2 + 1))
                            par = (gsl % 2) * HCN * 256
                            for th in range(2):
                                gi = (2 * (gsl * 2 + th)) % 6
                                ui = gi + 1
                                bwait(t, gi)
                                bwait(t, ui)
                                for hc in range(HCN):
                                    ins = t.matmul(
                                        pb[gi][:, :],
                                        wgu[:, par + hc * 256: par + hc * 256 + 128],
                                        bass.AP(xnT, hc * TQ + th * 512, [[HCN * TQ, 128], [1, 512]]),
                                        start=(hc == 0), stop=(hc == HCN - 1))
                                tn.inc(ins)
                                for hc in range(HCN):
                                    ins = t.matmul(
                                        pb[ui][:, :],
                                        wgu[:, par + hc * 256 + 128: par + hc * 256 + 256],
                                        bass.AP(xnT, hc * TQ + th * 512, [[HCN * TQ, 128], [1, 512]]),
                                        start=(hc == 0), stop=(hc == HCN - 1))
                                tn.inc(ins)
                        gu_done[gsl] = tn.val
                        gu_grp_tn = tn.val

                        silu_jobs.append((gsl, gu_grp_tn, fc, ex_down_tn_prev,
                                          (ei == 0 and fc == 0), xnh_done if False else 0))
                        silu_sc = sil_base + 2 * len(silu_jobs)

                        @b.vector
                        def _(v, gsl=gsl, fc=fc, silu_sc=silu_sc,
                              gu_grp_tn=gu_grp_tn):
                            for th in range(2):
                                gi = (2 * (gsl * 2 + th)) % 6
                                ui = gi + 1
                                v.wait_ge(sc.sem, silu_sc - 2 + (th + 1))
                                v.wait_ge(tn.sem, gu_grp_tn - 4 + (2 * th + 2))
                                sl = xnt[:, fc * TQ + th * 512: fc * TQ + (th + 1) * 512]
                                ins = v.tensor_tensor(sl, sl,
                                                      pb[ui][:, :], op=ALU.mult)
                                vc.inc(ins)
                                bank_guard[gi] = (sc.sem, silu_sc - 2 + (th + 1))
                                bank_guard[ui] = (vc.sem, vc.val)
                act_vc = vc.val

                for h4 in range(4):
                    dsl = q_dslab0 + ei * 4 + h4
                    with BLK() as b:
                        @b.gpsimd
                        def _(g, dsl=dsl):
                            issue_dn_upto(g, dsl + 1)

                        @b.tensor
                        def _(t, dsl=dsl, h4=h4, act_vc=act_vc, vc_base=vc.val):
                            t.wait_ge(d_wdn[dsl % 2].sem, 16 * (dsl // 2 + 1))
                            t.wait_ge(vc.sem, act_vc)
                            par = (dsl % 2) * NFC * 512
                            for tt in range(NTT):
                                di = (dsl * NTT + tt) % 6
                                if tt >= 6:
                                    t.wait_ge(vc.sem, vc_base + (tt - 6) + 1)
                                else:
                                    bwait(t, di)
                                for fc in range(NFC):
                                    ins = t.matmul(
                                        pb[di][:, :],
                                        xnt[:, fc * TQ + tt * 128: fc * TQ + tt * 128 + 128],
                                        wdn[:, par + fc * 512: par + (fc + 1) * 512],
                                        start=(fc == 0), stop=(fc == NFC - 1))
                                tn.inc(ins)
                        dn_done_here = tn.val
                        dn_done[dsl] = tn.val

                        @b.vector
                        def _(v, dsl=dsl, h4=h4, ei=ei, ex=ex, q=q,
                              dn_base=dn_done_here - NTT, comb_vc=comb_vc,
                              is_first=is_first):
                            v.wait_ge(vc.sem, comb_vc)
                            if is_first and h4 == 0:
                                v.wait_ge(d_out.sem, prev_out_dma)
                            for tt in range(NTT):
                                di = (dsl * NTT + tt) % 6
                                v.wait_ge(tn.sem, dn_base + tt + 1)
                                dst = ybuf[:, tt * D + h4 * 512: tt * D + (h4 + 1) * 512]
                                if ex == "sA":
                                    ins = v.tensor_copy(dst, pb[di][:, :])
                                elif ex == "sB":
                                    ins = v.scalar_tensor_tensor(dst, pb[di][:, :], 1.0,
                                                                 dst, op0=ALU.mult, op1=ALU.add)
                                else:
                                    ins = v.scalar_tensor_tensor(
                                        dst, pb[di][:, :], comb[:, tt * 8 + ex: tt * 8 + ex + 1],
                                        dst, op0=ALU.mult, op1=ALU.add)
                                vc.inc(ins)
                                bank_guard[di] = (vc.sem, vc.val)
                ex_down_tn_prev = tn.val
                if ei == len(ex_list) - 1:
                    y_tile_vc = [vc.val - NTT + t + 1 for t in range(NTT)]
            prev_down_tn = tn.val
            y_vc = vc.val

            with BLK() as b:
                @b.scalar
                def _(s, jobs=silu_jobs):
                    for gsl_j, grp_tn, fc_j, actwar, _f0, _ in jobs:
                        for th in range(2):
                            gi = (2 * (gsl_j * 2 + th)) % 6
                            s.wait_ge(tn.sem, max(grp_tn - 4 + (2 * th + 1), actwar))
                            sl = xnt[:, fc_j * TQ + th * 512: fc_j * TQ + (th + 1) * 512]
                            sc.inc(s.activation(sl, pb[gi][:, :], AF.Silu))
            assert sc.val == sil_base + 2 * len(silu_jobs)

            def w_stats4(v, _y=y_vc):
                v.wait_ge(vc.sem, _y)

            def w_apply4(s):
                pass
            with BLK() as b:
                @b.vector
                def _(v, _ytv=y_tile_vc):
                    for t in range(NTT):
                        v.wait_ge(vc.sem, _ytv[t])
                        for cch in range(4):
                            v.bn_stats(bnst[:, t * 24 + cch * 6: t * 24 + cch * 6 + 6],
                                       ybuf[:, t * D + cch * 512: t * D + (cch + 1) * 512])
                    vc.inc(v.drain())
                    v.wait_ge(vc.sem, vc.val)
                    for t in range(NTT):
                        v.bn_aggr(mv[:, 2 * t:2 * t + 2], bnst[:, t * 24:(t + 1) * 24])
                    vc.inc(v.drain())
                    v.wait_ge(vc.sem, vc.val)
                    var_ap = bass.AP(mv, 1, [[2 * NTT, 128], [2, NTT]])
                    vc.inc(v.tensor_scalar(rstd[:, :], var_ap, EPS, None, op0=ALU.add))
                    v.wait_ge(vc.sem, vc.val)
                    vc.inc(v.reciprocal(rstd[:, :], rstd[:, :]))
                v_rec4 = vc.val

                @b.scalar
                def _(s):
                    s.wait_ge(vc.sem, v_rec4)
                    sc.inc(s.activation(rstd[:, :], rstd[:, :], AF.Sqrt))
                    s.wait_ge(vc.sem, v_rec4 + 2)
                    for t in range(NTT):
                        if t == 0:
                            s.wait_ge(tn.sem, prev_down_tn)
                        ins = s.activation(xt(xnt, t), xt(x1b, t), AF.Identity,
                                           bias=nmr[:, t:t + 1])
                        sc.inc(ins)
                fin1_done = sc.val
                s_sq4 = fin1_done - NTT

                @b.vector
                def _(v):
                    v.wait_ge(sc.sem, s_sq4)
                    mean_ap = bass.AP(mv, 0, [[2 * NTT, 128], [2, NTT]])
                    vc.inc(v.tensor_tensor(nmr[:, :], mean_ap, rstd[:, :], op=ALU.mult))
                    v.wait_ge(vc.sem, vc.val)
                    vc.inc(v.tensor_scalar(nmr[:, :], nmr[:, :], -1.0, None, op0=ALU.mult))
                assert vc.val == v_rec4 + 2

                @b.vector
                def _(v):
                    for t in range(NTT):
                        v.wait_ge(sc.sem, fin1_done - NTT + t + 1)
                        ins = v.scalar_tensor_tensor(xt(ybuf, t), xt(ybuf, t),
                                                     rstd[:, t:t + 1], xt(xnt, t),
                                                     op0=ALU.mult, op1=ALU.add)
                        vc.inc(ins)
                out_vc = vc.val

                @b.sync
                def _(s):
                    base = out_vc - NTT
                    for t in range(NTT):
                        s.wait_ge(vc.sem, base + t + 1)
                        dst = bass.AP(out.tensor, (q * TQ + t * 128) * D,
                                      [[D, 128], [1, D]])
                        d_out.inc(s.dma_start(dst, xt(ybuf, t)), 16)
            prev_out_dma = d_out.val
            prev_q_attn_usersc = sc.val

        with BLK() as b:
            @b.sync
            def _(s):
                s.wait_ge(d_out.sem, d_out.val)

    nc.finalize()
    return nc


_IDENT = np.eye(128, dtype=ml_dtypes.bfloat16)
_SEG_PERM = np.concatenate([np.arange(0, 512, 2), np.arange(1, 512, 2)])
_FULL_PERM = np.concatenate(
    [b * 2048 + s * 512 + _SEG_PERM for b in range(2) for s in range(4)])


def kernel(L0_attn_x, L0_ffn_gate_weight, L0_ffn_gate_up_weights,
           L0_ffn_down_weights, L0_ffn_shared_gate_up, L0_ffn_shared_down):
    x = np.ascontiguousarray(L0_attn_x, dtype=np.float32)
    wguT = np.ascontiguousarray(
        np.asarray(L0_ffn_gate_up_weights, dtype=np.float32).transpose(2, 0, 1))
    wdnT = np.ascontiguousarray(
        np.asarray(L0_ffn_down_weights, dtype=np.float32).transpose(0, 2, 1))
    sguT = np.ascontiguousarray(
        np.asarray(L0_ffn_shared_gate_up, dtype=np.float32).T)
    sdnT = np.ascontiguousarray(
        np.asarray(L0_ffn_shared_down, dtype=np.float32).T)
    gateT = np.ascontiguousarray(
        np.asarray(L0_ffn_gate_weight, dtype=np.float32).T)

    nc = build_program(NQ)
    in_maps = []
    for c in range(8):
        xc = x[2 * c:2 * c + 2].reshape(TCORE, D)[_FULL_PERM]
        in_maps.append({"x": np.ascontiguousarray(xc), "wguT": wguT,
                        "wdnT": wdnT, "sguT": sguT, "sdnT": sdnT,
                        "gateT": gateT, "ident128": _IDENT})
    res = run_bass_kernel_spmd(nc, in_maps, core_ids=list(range(8)))
    outs = []
    for c in range(8):
        oc = np.asarray(res.results[c]["out"])
        inv = np.empty((TCORE, D), dtype=np.float32)
        inv[_FULL_PERM] = oc
        outs.append(inv.reshape(2, 2048, D))
    return np.concatenate(outs, axis=0)


# version 13
# speedup vs baseline: 1.1185x; 1.1185x over previous
import numpy as np
import ml_dtypes
from contextlib import ExitStack
import concourse.bass as bass
import concourse.mybir as mybir
from concourse.bass_utils import run_bass_kernel_spmd

F32 = mybir.dt.float32
BF16 = mybir.dt.bfloat16
I32 = mybir.dt.int32
AF = mybir.ActivationFunctionType
ALU = mybir.AluOpType
AX = mybir.AxisListType

D = 2048
HCN = 16
TQ = 1024
NTT = 8
NQ = 4
NH = 16
HD = 128
F = 1408
NFC = 11
NE = 8
FS = 2816
EPS = 1e-5
SCL = 128.0 ** -0.5
TCORE = 4096
LA = 2


class Ctr:
    def __init__(self, sem):
        self.sem = sem
        self.val = 0

    def inc(self, inst, n=1):
        inst.then_inc(self.sem, n)
        self.val += n
        return self.val


def build_program(nq=NQ):
    nc = bass.Bass("TRN2", target_bir_lowering=False)

    x_in = nc.dram_tensor("x", [TCORE, D], F32, kind="ExternalInput").ap()
    wguT = nc.dram_tensor("wguT", [D, NE, 2 * F], F32, kind="ExternalInput").ap()
    wdnT = nc.dram_tensor("wdnT", [NE, F, D], F32, kind="ExternalInput").ap()
    sguT = nc.dram_tensor("sguT", [D, 2 * FS], F32, kind="ExternalInput").ap()
    sdnT = nc.dram_tensor("sdnT", [FS, D], F32, kind="ExternalInput").ap()
    gateT = nc.dram_tensor("gateT", [D, NE], F32, kind="ExternalInput").ap()
    ident_h = nc.dram_tensor("ident128", [128, 128], BF16, kind="ExternalInput").ap()
    out = nc.dram_tensor("out", [TCORE, D], F32, kind="ExternalOutput").ap()

    with ExitStack() as ctx:
        En = ctx.enter_context

        ybuf = En(nc.sbuf_tensor("ybuf", [128, NTT * D], F32))
        x1b = En(nc.sbuf_tensor("x1b", [128, NTT * D], BF16))
        xnt = En(nc.sbuf_tensor("xnt", [128, NTT * D], BF16))
        xnT = En(nc.sbuf_tensor("xnT", [128, HCN * TQ], BF16))
        wgu = En(nc.sbuf_tensor("wgu", [128, 2 * HCN * 256], BF16))
        wdn = En(nc.sbuf_tensor("wdn", [128, 2 * NFC * 512], BF16))
        gts = En(nc.sbuf_tensor("gts", [128, HCN * 8], BF16))
        ident = En(nc.sbuf_tensor("ident", [128, 128], BF16))
        bnst = En(nc.sbuf_tensor("bnst", [128, NTT * 24], F32))
        mv = En(nc.sbuf_tensor("mv", [128, NTT * 2], F32))
        rstd = En(nc.sbuf_tensor("rstd", [128, NTT], F32))
        nmr = En(nc.sbuf_tensor("nmr", [128, NTT], F32))
        lgs = En(nc.sbuf_tensor("lgs", [128, 64], F32))
        gsc = En(nc.sbuf_tensor("gsc", [128, 64], F32))
        selm = En(nc.sbuf_tensor("selm", [128, 64], F32))
        comb = En(nc.sbuf_tensor("comb", [128, 64], F32))
        gst = En(nc.sbuf_tensor("gst", [128, 24], F32))
        smx = En(nc.sbuf_tensor("smx", [128, 24], F32))
        expb = En(nc.sbuf_tensor("expb", [128, 3 * 512], BF16))
        attT = En(nc.sbuf_tensor("attT", [128, 3 * 512], BF16))

        pb = [En(nc.psum_tensor(f"pb{i}", [128, 512], F32)) for i in range(6)]
        tb = [En(nc.psum_tensor(f"tb{i}", [128, 1024], BF16)) for i in range(2)]

        tn = Ctr(En(nc.semaphore("tn")))
        vc = Ctr(En(nc.semaphore("vc")))
        sc = Ctr(En(nc.semaphore("sc")))
        gp = Ctr(En(nc.semaphore("gp")))
        d_wgu = [Ctr(En(nc.semaphore("d_wgu0"))), Ctr(En(nc.semaphore("d_wgu1")))]
        d_wdn = [Ctr(En(nc.semaphore("d_wdn0"))), Ctr(En(nc.semaphore("d_wdn1")))]
        d_xin = Ctr(En(nc.semaphore("d_xin")))
        d_xt = [Ctr(En(nc.semaphore(f"d_xt{i}"))) for i in range(NTT)]
        d_gt = Ctr(En(nc.semaphore("d_gt")))
        d_out = Ctr(En(nc.semaphore("d_out")))

        from contextlib import contextmanager
        main_block = En(nc.Block(no_gpsimd_drain=True))

        @contextmanager
        def BLK():
            yield main_block

        ex_list = ["sA", "sB"] + list(range(NE))

        def gu_srcs(ex, fc):
            if ex == "sA":
                t, og, ou = sguT.tensor, fc * 128, 2 * F + fc * 128
                dims = [[2 * FS, 128], [128 * 2 * FS, HCN], [1, 128]]
            elif ex == "sB":
                t, og, ou = sguT.tensor, F + fc * 128, 2 * F + F + fc * 128
                dims = [[2 * FS, 128], [128 * 2 * FS, HCN], [1, 128]]
            else:
                t = wguT.tensor
                og = ex * 2 * F + fc * 128
                ou = ex * 2 * F + F + fc * 128
                dims = [[NE * 2 * F, 128], [128 * NE * 2 * F, HCN], [1, 128]]
            return (bass.AP(t, og, [list(d) for d in dims]),
                    bass.AP(t, ou, [list(d) for d in dims]))

        def dn_src(ex, h4):
            h0 = h4 * 512
            if ex == "sA":
                return bass.AP(sdnT.tensor, h0, [[D, 128], [128 * D, NFC], [1, 512]])
            if ex == "sB":
                return bass.AP(sdnT.tensor, F * D + h0,
                               [[D, 128], [128 * D, NFC], [1, 512]])
            return bass.AP(wdnT.tensor, ex * F * D + h0,
                           [[D, 128], [128 * D, NFC], [1, 512]])

        gu_q = [(ex, fc) for _ in range(nq) for ex in ex_list for fc in range(NFC)]
        dn_q = [(ex, h4) for _ in range(nq) for ex in ex_list for h4 in range(4)]
        gu_done = {}
        dn_done = {}
        gu_issued = [0]
        dn_issued = [0]

        def issue_gu_upto(g, j):
            while gu_issued[0] <= min(j, len(gu_q) - 1):
                k = gu_issued[0]
                if k - 2 >= 0:
                    g.wait_ge(tn.sem, gu_done[k - 2])
                ga, ua = gu_srcs(*gu_q[k])
                par = (k % 2) * HCN * 256
                dg = bass.AP(wgu, par, [[2 * HCN * 256, 128], [256, HCN], [1, 128]])
                du = bass.AP(wgu, par + 128, [[2 * HCN * 256, 128], [256, HCN], [1, 128]])
                d_wgu[k % 2].inc(g.dma_start(dg, ga), 16)
                d_wgu[k % 2].inc(g.dma_start(du, ua), 16)
                gu_issued[0] += 1

        def issue_dn_upto(g, j):
            while dn_issued[0] <= min(j, len(dn_q) - 1):
                k = dn_issued[0]
                if k - 2 >= 0:
                    g.wait_ge(tn.sem, dn_done[k - 2])
                sa = dn_src(*dn_q[k])
                par = (k % 2) * NFC * 512
                dd = bass.AP(wdn, par, [[2 * NFC * 512, 128], [512, NFC], [1, 512]])
                d_wdn[k % 2].inc(g.dma_start(dd, sa), 16)
                dn_issued[0] += 1

        bank_guard = [None] * 6
        tb_guard = [None] * 2

        def bwait(t, i):
            if bank_guard[i] is not None:
                t.wait_ge(bank_guard[i][0], bank_guard[i][1])

        def tbwait(t, i):
            if tb_guard[i] is not None:
                t.wait_ge(tb_guard[i][0], tb_guard[i][1])

        with BLK() as b:
            @b.gpsimd
            def _(g):
                src = bass.AP(gateT.tensor, 0, [[NE, 128], [128 * NE, HCN], [1, NE]])
                d_gt.inc(g.dma_start(gts[:, :], src), 16)
                d_gt.inc(g.dma_start(ident[:, :], ident_h[:, :]), 16)
                for t in range(NTT):
                    srcx = bass.AP(x_in.tensor, t * 128 * D, [[D, 128], [1, D]])
                    d_xin.inc(g.dma_start(x1b[:, t * D:(t + 1) * D], srcx), 16)
                issue_gu_upto(g, LA - 1)
                issue_dn_upto(g, 0)

        def xt(buf, t):
            return buf[:, t * D:(t + 1) * D]

        def ln_site(src_buf, dst_buf, dst_dtype_note, wait_before_stats,
                    wait_before_apply, post_apply_inc, src_bf=False,
                    per_tile_stat_wait=None):
            with BLK() as b:
                @b.vector
                def _(v):
                    wait_before_stats(v)
                    for t in range(NTT):
                        if per_tile_stat_wait is not None:
                            per_tile_stat_wait(v, t)
                        for cch in range(4):
                            v.bn_stats(bnst[:, t * 24 + cch * 6: t * 24 + cch * 6 + 6],
                                       src_buf[:, t * D + cch * 512: t * D + (cch + 1) * 512])
                    vc.inc(v.drain())
                    v.wait_ge(vc.sem, vc.val)
                    for t in range(NTT):
                        v.bn_aggr(mv[:, 2 * t:2 * t + 2], bnst[:, t * 24:(t + 1) * 24])
                    vc.inc(v.drain())
                    v.wait_ge(vc.sem, vc.val)
                    var_ap = bass.AP(mv, 1, [[2 * NTT, 128], [2, NTT]])
                    vc.inc(v.tensor_scalar(rstd[:, :], var_ap, EPS, None, op0=ALU.add))
                    v.wait_ge(vc.sem, vc.val)
                    vc.inc(v.reciprocal(rstd[:, :], rstd[:, :]))
                v_rec = vc.val

                @b.scalar
                def _(s):
                    s.wait_ge(vc.sem, v_rec)
                    sc.inc(s.activation(rstd[:, :], rstd[:, :], AF.Sqrt))
                    s.wait_ge(vc.sem, v_rec + 2)
                    wait_before_apply(s)
                    for t in range(NTT):
                        ins = s.activation(xt(dst_buf, t), xt(src_buf, t), AF.Identity,
                                           bias=nmr[:, t:t + 1], scale=rstd[:, t:t + 1])
                        sc.inc(ins)
                s_sqrt = sc.val - NTT

                @b.vector
                def _(v):
                    v.wait_ge(sc.sem, s_sqrt)
                    mean_ap = bass.AP(mv, 0, [[2 * NTT, 128], [2, NTT]])
                    vc.inc(v.tensor_tensor(nmr[:, :], mean_ap, rstd[:, :], op=ALU.mult))
                    v.wait_ge(vc.sem, vc.val)
                    vc.inc(v.tensor_scalar(nmr[:, :], nmr[:, :], -1.0, None, op0=ALU.mult))
                assert vc.val == v_rec + 2
            return sc.val

        def emit_xn_transposes(apply_base):
            with BLK() as b:
                ev_base = vc.val

                @b.tensor
                def _(t, apply_base=apply_base, ev_base=ev_base):
                    t.wait_ge(d_gt.sem, 32)
                    for tt in range(NTT):
                        t.wait_ge(sc.sem, apply_base + tt + 1)
                        for half in range(2):
                            tbwait(t, half)
                            for hcl in range(8):
                                hc = half * 8 + hcl
                                ins = t.transpose(
                                    tb[half][:, hcl * 128:(hcl + 1) * 128],
                                    xnt[:, tt * D + hc * 128: tt * D + hc * 128 + 128],
                                    ident[:, :])
                            tn.inc(ins)
                            tb_guard[half] = (vc.sem, ev_base + 2 * tt + half + 1)

                @b.vector
                def _(v, ev_base=ev_base):
                    base = tn.val - 2 * NTT
                    for tt in range(NTT):
                        for half in range(2):
                            v.wait_ge(tn.sem, base + 2 * tt + half + 1)
                            for hcl in range(8):
                                hc = half * 8 + hcl
                                ins = v.tensor_copy(
                                    xnT[:, hc * TQ + tt * 128: hc * TQ + (tt + 1) * 128],
                                    tb[half][:, hcl * 128:(hcl + 1) * 128])
                            vc.inc(ins)
            assert vc.val == ev_base + 2 * NTT
            return ev_base

        fin1_done = 0
        prev_out_dma = 0
        prev_down_tn = 0
        prev_q_attn_usersc = 0

        for q in range(nq):
            if q > 0:
                with BLK() as b:
                    @b.gpsimd
                    def _(g):
                        for t in range(NTT):
                            g.wait_ge(sc.sem, fin1_done - NTT + t + 1)
                            srcx = bass.AP(x_in.tensor, (q * TQ + t * 128) * D,
                                           [[D, 128], [1, D]])
                            d_xt[t].inc(g.dma_start(x1b[:, t * D:(t + 1) * D], srcx), 16)
            xt_done = [c.val for c in d_xt]

            def w_stats1(v, _q=q):
                if _q == 0:
                    v.wait_ge(d_xin.sem, d_xin.val)
                else:
                    v.wait_ge(tn.sem, prev_down_tn)

            def ptw1(v, t, _q=q, _xt=xt_done):
                if _q > 0:
                    v.wait_ge(d_xt[t].sem, _xt[t])

            def w_apply1(s, _q=q):
                if _q > 0:
                    s.wait_ge(sc.sem, fin1_done)
            ln1_sc = ln_site(x1b, xnt, None, w_stats1, w_apply1, None, src_bf=True,
                             per_tile_stat_wait=ptw1)

            with BLK() as b:
                @b.vector
                def _(v):
                    if q > 0:
                        v.wait_ge(d_out.sem, prev_out_dma)
                    vc.inc(v.memset(ybuf[:, :], 0.0))
            memset_done = vc.val

            ev1_base = emit_xn_transposes(ln1_sc - NTT)
            xnT1_vc = vc.val

            smax_slot = 0
            attn_sbase = sc.val
            sc_jobs = []
            for ls in range(2):
                for hc in range(NH):
                    par = hc % 2
                    g_off = ls * 512 + par * 256
                    tA = 4 * ls + 2 * par
                    it = ls * NH + hc
                    tbi = it % 2
                    ebuf = it % 3
                    qk_bank_i = (0, 1, 4)[it % 3]
                    av_bank_i = (2, 3, 5)[it % 3]
                    qkb = pb[qk_bank_i]
                    avb = pb[av_bank_i]
                    exp_sc = attn_sbase + 2 * it + 1
                    avev_sc = attn_sbase + 2 * it + 2
                    with BLK() as b:
                        @b.tensor
                        def _(t, hc=hc, ls=ls, g_off=g_off, qk_bank_i=qk_bank_i,
                              qkb=qkb, ev1_base=ev1_base):
                            t.wait_ge(vc.sem,
                                      ev1_base + 2 * (4 * ls + 3) + (hc // 8) + 1)
                            bwait(t, qk_bank_i)
                            g_ap = xnT[:, hc * TQ + g_off: hc * TQ + g_off + 256]
                            for qc in range(2):
                                ins = t.matmul(qkb[:, qc * 256:(qc + 1) * 256],
                                               xnT[:, hc * TQ + g_off + qc * 128:
                                                   hc * TQ + g_off + qc * 128 + 128],
                                               g_ap, start=True, stop=True)
                            tn.inc(ins)
                        qk_tn = tn.val

                        @b.vector
                        def _(v, qkb=qkb, slot=smax_slot, qk_tn=qk_tn):
                            v.wait_ge(tn.sem, qk_tn)
                            v.reduce_max(smx[:, slot: slot + 2],
                                         bass.AP(qkb, 0, [[512, 128], [256, 2], [1, 256]]),
                                         axis=AX.X, negate=True).then_inc(vc.sem, 1)
                            vc.val += 1
                            v.wait_ge(vc.sem, vc.val)
                            vc.inc(v.tensor_scalar(smx[:, slot + 2: slot + 4],
                                                   smx[:, slot: slot + 2],
                                                   SCL, None, op0=ALU.mult))
                        nm_vc = vc.val

                        sc_jobs.append(("exp", qkb, smax_slot, ebuf, nm_vc))
                        bank_guard[qk_bank_i] = (sc.sem, exp_sc)

                        @b.vector
                        def _(v, slot=smax_slot, ebuf=ebuf, exp_sc=exp_sc):
                            v.wait_ge(sc.sem, exp_sc)
                            v.reduce_sum(smx[:, slot + 4: slot + 6],
                                         bass.AP(expb, ebuf * 512,
                                                 [[3 * 512, 128], [256, 2], [1, 256]]),
                                         axis=AX.X).then_inc(vc.sem, 1)
                            vc.val += 1
                            v.wait_ge(vc.sem, vc.val)
                            vc.inc(v.reciprocal(smx[:, slot + 6: slot + 8],
                                                smx[:, slot + 4: slot + 6]))
                        rinv_vc = vc.val

                        @b.tensor
                        def _(t, ebuf=ebuf, tbi=tbi, exp_sc=exp_sc):
                            t.wait_ge(sc.sem, exp_sc)
                            tbwait(t, tbi)
                            for kc in range(2):
                                for qc in range(2):
                                    ins = t.transpose(
                                        tb[tbi][:, kc * 256 + qc * 128:
                                                kc * 256 + qc * 128 + 128],
                                        expb[:, ebuf * 512 + qc * 256 + kc * 128:
                                             ebuf * 512 + qc * 256 + kc * 128 + 128],
                                        ident[:, :])
                            tn.inc(ins)
                        trp_tn = tn.val
                        tb_guard[tbi] = (vc.sem, vc.val + 1)

                        @b.vector
                        def _(v, ebuf=ebuf, tbi=tbi, trp_tn=trp_tn):
                            v.wait_ge(tn.sem, trp_tn)
                            vc.inc(v.tensor_copy(attT[:, ebuf * 512:(ebuf + 1) * 512],
                                                 tb[tbi][:, 0:512]))
                        att_vc = vc.val

                        @b.tensor
                        def _(t, hc=hc, tA=tA, ebuf=ebuf, av_bank_i=av_bank_i,
                              avb=avb, att_vc=att_vc):
                            t.wait_ge(vc.sem, att_vc)
                            bwait(t, av_bank_i)
                            for qc in range(2):
                                for kc in range(2):
                                    ins = t.matmul(
                                        avb[:, qc * 128:(qc + 1) * 128],
                                        attT[:, ebuf * 512 + kc * 256 + qc * 128:
                                             ebuf * 512 + kc * 256 + qc * 128 + 128],
                                        xnt[:, (tA + kc) * D + hc * 128:
                                            (tA + kc) * D + hc * 128 + 128],
                                        start=(kc == 0), stop=(kc == 1))
                            tn.inc(ins)
                        av_tn = tn.val

                        sc_jobs.append(("avev", avb, smax_slot, (tA, hc),
                                        (av_tn, rinv_vc, memset_done)))
                        bank_guard[av_bank_i] = (sc.sem, avev_sc)
                    smax_slot = (smax_slot + 8) % 24
            attn_vc = vc.val

            with BLK() as b:
                @b.scalar
                def _(s, jobs=sc_jobs):
                    for kind, bankt, slot, aux, waits in [
                            (j[0], j[1], j[2], j[3], j[4]) for j in jobs]:
                        if kind == "exp":
                            ebuf, nm_vc = aux, waits
                            s.wait_ge(vc.sem, nm_vc)
                            for qc in range(2):
                                ins = s.activation(
                                    expb[:, ebuf * 512 + qc * 256:
                                         ebuf * 512 + (qc + 1) * 256],
                                    bankt[:, qc * 256:(qc + 1) * 256], AF.Exp,
                                    bias=smx[:, slot + 2 + qc: slot + 3 + qc],
                                    scale=SCL)
                            sc.inc(ins)
                        else:
                            tA, hc = aux
                            av_tn, rinv_vc, msd = waits
                            s.wait_ge(tn.sem, av_tn)
                            s.wait_ge(vc.sem, rinv_vc)
                            s.wait_ge(vc.sem, msd)
                            for qc in range(2):
                                ins = s.activation(
                                    ybuf[:, (tA + qc) * D + hc * 128:
                                         (tA + qc) * D + hc * 128 + 128],
                                    bankt[:, qc * 128:(qc + 1) * 128], AF.Identity,
                                    scale=smx[:, slot + 6 + qc: slot + 7 + qc])
                            sc.inc(ins)
            assert sc.val == attn_sbase + 4 * NH
            attn_done_sc = sc.val

            def w_stats2(v, _m=memset_done):
                v.wait_ge(vc.sem, _m)

            def ptw2(v, t, _b=attn_sbase):
                ls = t // 4
                par = (t // 2) % 2
                it = ls * NH + (14 + par)
                v.wait_ge(sc.sem, _b + 2 * it + 2)

            def w_apply2(s):
                pass
            ln2_sc = ln_site(ybuf, xnt, None, w_stats2, w_apply2, None,
                             per_tile_stat_wait=ptw2)
            res1_base = vc.val
            with BLK() as b:
                @b.vector
                def _(v):
                    for t in range(NTT):
                        v.wait_ge(sc.sem, ln2_sc - NTT + t + 1)
                        ins = v.tensor_tensor(xt(x1b, t), xt(x1b, t), xt(xnt, t),
                                              op=ALU.add)
                        vc.inc(ins)
            res1_vc = vc.val

            def w_stats3(v):
                pass

            def w_apply3(s):
                pass

            def ptw3(v, t, _b=res1_base):
                v.wait_ge(vc.sem, _b + t + 1)
            ln3_sc = ln_site(x1b, xnt, None, w_stats3, w_apply3, None, src_bf=True,
                             per_tile_stat_wait=ptw3)

            emit_xn_transposes(ln3_sc - NTT)
            xnT2_vc = vc.val

            with BLK() as b:
                @b.tensor
                def _(t):
                    t.wait_ge(vc.sem, xnT2_vc)
                    t.wait_ge(d_gt.sem, 32)
                    bwait(t, 0)
                    for tt in range(NTT):
                        for hc in range(HCN):
                            ins = t.matmul(pb[0][:, tt * 8:(tt + 1) * 8],
                                           xnT[:, hc * TQ + tt * 128: hc * TQ + tt * 128 + 128],
                                           gts[:, hc * 8:(hc + 1) * 8],
                                           start=(hc == 0), stop=(hc == HCN - 1))
                    tn.inc(ins)
                gat_tn = tn.val

                @b.vector
                def _(v):
                    v.wait_ge(tn.sem, gat_tn)
                    vc.inc(v.tensor_copy(lgs[:, :], pb[0][:, 0:64]))
                    bank_guard[0] = (vc.sem, vc.val)
                    v.wait_ge(vc.sem, vc.val)
                    seg = lambda buf, o=0: bass.AP(buf, o, [[64, 128], [8, 8], [1, 8]])
                    vc.inc(v.reduce_max(gst[:, 0:8], seg(lgs), axis=AX.X, negate=True))
                    v.wait_ge(vc.sem, vc.val)
                    nmx1b = bass.AP(gst, 0, [[24, 128], [1, 8], [0, 8]])
                    vc.inc(v.tensor_tensor(seg(gsc), seg(lgs), nmx1b, op=ALU.add))
                    v.wait_ge(vc.sem, vc.val)
                    vc.inc(v.tensor_scalar(selm[:, :], gsc[:, :], 0.0, None, op0=ALU.is_ge))
                    v.wait_ge(vc.sem, vc.val)
                    vc.inc(v.scalar_tensor_tensor(selm[:, :], selm[:, :], -1e9, lgs[:, :],
                                                  op0=ALU.mult, op1=ALU.add))
                    v.wait_ge(vc.sem, vc.val)
                    vc.inc(v.reduce_max(gst[:, 8:16], seg(selm), axis=AX.X, negate=True))
                    v.wait_ge(vc.sem, vc.val)
                    nmx2b = bass.AP(gst, 8, [[24, 128], [1, 8], [0, 8]])
                    vc.inc(v.tensor_tensor(seg(selm), seg(lgs), nmx2b, op=ALU.add))
                    v.wait_ge(vc.sem, vc.val)
                    vc.inc(v.tensor_scalar(selm[:, :], selm[:, :], 0.0, None, op0=ALU.is_ge))
                gsub_vc = vc.val

                @b.scalar
                def _(s):
                    s.wait_ge(vc.sem, gsub_vc)
                    sc.inc(s.activation(comb[:, :], gsc[:, :], AF.Exp))
                gexp_sc = sc.val

                @b.vector
                def _(v):
                    v.wait_ge(sc.sem, gexp_sc)
                    seg = lambda buf, o=0: bass.AP(buf, o, [[64, 128], [8, 8], [1, 8]])
                    vc.inc(v.reduce_sum(gst[:, 16:24], seg(comb), axis=AX.X))
                    v.wait_ge(vc.sem, vc.val)
                    vc.inc(v.reciprocal(gst[:, 16:24], gst[:, 16:24]))
                    v.wait_ge(vc.sem, vc.val)
                    rinvb = bass.AP(gst, 16, [[24, 128], [1, 8], [0, 8]])
                    vc.inc(v.tensor_tensor(seg(comb), seg(comb), rinvb, op=ALU.mult))
                    v.wait_ge(vc.sem, vc.val)
                    vc.inc(v.tensor_tensor(comb[:, :], comb[:, :], selm[:, :], op=ALU.mult))
            comb_vc = vc.val

            q_slab0 = q * len(ex_list) * NFC
            q_dslab0 = q * len(ex_list) * 4
            ex_down_tn_prev = prev_down_tn
            sil_base = sc.val
            silu_jobs = []
            for ei, ex in enumerate(ex_list):
                is_first = (ei == 0)
                for fc in range(NFC):
                    gsl = q_slab0 + ei * NFC + fc
                    grp = (gsl * 2) % 3
                    with BLK() as b:
                        @b.gpsimd
                        def _(g, gsl=gsl, fc=fc, ei=ei):
                            issue_gu_upto(g, gsl + 1)
                            if fc == 0:
                                issue_dn_upto(g, q_dslab0 + ei * 4)

                        @b.tensor
                        def _(t, gsl=gsl, fc=fc, ei=ei,
                              _actwar=ex_down_tn_prev):
                            t.wait_ge(d_wgu[gsl % 2].sem, 32 * (gsl // 2 + 1))
                            par = (gsl % 2) * HCN * 256
                            for th in range(2):
                                gi = (2 * (gsl * 2 + th)) % 6
                                ui = gi + 1
                                bwait(t, gi)
                                bwait(t, ui)
                                for hc in range(HCN):
                                    ins = t.matmul(
                                        pb[gi][:, :],
                                        wgu[:, par + hc * 256: par + hc * 256 + 128],
                                        bass.AP(xnT, hc * TQ + th * 512, [[HCN * TQ, 128], [1, 512]]),
                                        start=(hc == 0), stop=(hc == HCN - 1))
                                tn.inc(ins)
                                for hc in range(HCN):
                                    ins = t.matmul(
                                        pb[ui][:, :],
                                        wgu[:, par + hc * 256 + 128: par + hc * 256 + 256],
                                        bass.AP(xnT, hc * TQ + th * 512, [[HCN * TQ, 128], [1, 512]]),
                                        start=(hc == 0), stop=(hc == HCN - 1))
                                tn.inc(ins)
                        gu_done[gsl] = tn.val
                        gu_grp_tn = tn.val

                        silu_jobs.append((gsl, gu_grp_tn, fc, ex_down_tn_prev,
                                          (ei == 0 and fc == 0), 0))
                        silu_sc = sil_base + 2 * len(silu_jobs)

                        @b.vector
                        def _(v, gsl=gsl, fc=fc, silu_sc=silu_sc,
                              gu_grp_tn=gu_grp_tn):
                            for th in range(2):
                                gi = (2 * (gsl * 2 + th)) % 6
                                ui = gi + 1
                                v.wait_ge(sc.sem, silu_sc - 2 + (th + 1))
                                v.wait_ge(tn.sem, gu_grp_tn - 4 + (2 * th + 2))
                                sl = xnt[:, fc * TQ + th * 512: fc * TQ + (th + 1) * 512]
                                ins = v.tensor_tensor(sl, sl,
                                                      pb[ui][:, :], op=ALU.mult)
                                vc.inc(ins)
                                bank_guard[gi] = (sc.sem, silu_sc - 2 + (th + 1))
                                bank_guard[ui] = (vc.sem, vc.val)
                act_vc = vc.val

                for h4 in range(4):
                    dsl = q_dslab0 + ei * 4 + h4
                    with BLK() as b:
                        @b.gpsimd
                        def _(g, dsl=dsl):
                            issue_dn_upto(g, dsl + 1)

                        @b.tensor
                        def _(t, dsl=dsl, h4=h4, act_vc=act_vc, vc_base=vc.val):
                            t.wait_ge(d_wdn[dsl % 2].sem, 16 * (dsl // 2 + 1))
                            t.wait_ge(vc.sem, act_vc)
                            par = (dsl % 2) * NFC * 512
                            for tt in range(NTT):
                                di = (dsl * NTT + tt) % 6
                                if tt >= 6:
                                    t.wait_ge(vc.sem, vc_base + (tt - 6) + 1)
                                else:
                                    bwait(t, di)
                                for fc in range(NFC):
                                    ins = t.matmul(
                                        pb[di][:, :],
                                        xnt[:, fc * TQ + tt * 128: fc * TQ + tt * 128 + 128],
                                        wdn[:, par + fc * 512: par + (fc + 1) * 512],
                                        start=(fc == 0), stop=(fc == NFC - 1))
                                tn.inc(ins)
                        dn_done_here = tn.val
                        dn_done[dsl] = tn.val

                        @b.vector
                        def _(v, dsl=dsl, h4=h4, ei=ei, ex=ex, q=q,
                              dn_base=dn_done_here - NTT, comb_vc=comb_vc,
                              is_first=is_first):
                            v.wait_ge(vc.sem, comb_vc)
                            if is_first and h4 == 0:
                                v.wait_ge(d_out.sem, prev_out_dma)
                            for tt in range(NTT):
                                di = (dsl * NTT + tt) % 6
                                v.wait_ge(tn.sem, dn_base + tt + 1)
                                dst = ybuf[:, tt * D + h4 * 512: tt * D + (h4 + 1) * 512]
                                if ex == "sA":
                                    ins = v.tensor_copy(dst, pb[di][:, :])
                                elif ex == "sB":
                                    ins = v.scalar_tensor_tensor(dst, pb[di][:, :], 1.0,
                                                                 dst, op0=ALU.mult, op1=ALU.add)
                                else:
                                    ins = v.scalar_tensor_tensor(
                                        dst, pb[di][:, :], comb[:, tt * 8 + ex: tt * 8 + ex + 1],
                                        dst, op0=ALU.mult, op1=ALU.add)
                                vc.inc(ins)
                                bank_guard[di] = (vc.sem, vc.val)
                ex_down_tn_prev = tn.val
                if ei == len(ex_list) - 1:
                    y_tile_vc = [vc.val - NTT + t + 1 for t in range(NTT)]
            prev_down_tn = tn.val
            y_vc = vc.val

            with BLK() as b:
                @b.scalar
                def _(s, jobs=silu_jobs):
                    for gsl_j, grp_tn, fc_j, actwar, _f0, _ in jobs:
                        for th in range(2):
                            gi = (2 * (gsl_j * 2 + th)) % 6
                            s.wait_ge(tn.sem, max(grp_tn - 4 + (2 * th + 1), actwar))
                            sl = xnt[:, fc_j * TQ + th * 512: fc_j * TQ + (th + 1) * 512]
                            sc.inc(s.activation(sl, pb[gi][:, :], AF.Silu))
            assert sc.val == sil_base + 2 * len(silu_jobs)

            def w_stats4(v, _y=y_vc):
                v.wait_ge(vc.sem, _y)

            def w_apply4(s):
                pass
            with BLK() as b:
                @b.vector
                def _(v, _ytv=y_tile_vc):
                    for t in range(NTT):
                        v.wait_ge(vc.sem, _ytv[t])
                        for cch in range(4):
                            v.bn_stats(bnst[:, t * 24 + cch * 6: t * 24 + cch * 6 + 6],
                                       ybuf[:, t * D + cch * 512: t * D + (cch + 1) * 512])
                    vc.inc(v.drain())
                    v.wait_ge(vc.sem, vc.val)
                    for t in range(NTT):
                        v.bn_aggr(mv[:, 2 * t:2 * t + 2], bnst[:, t * 24:(t + 1) * 24])
                    vc.inc(v.drain())
                    v.wait_ge(vc.sem, vc.val)
                    var_ap = bass.AP(mv, 1, [[2 * NTT, 128], [2, NTT]])
                    vc.inc(v.tensor_scalar(rstd[:, :], var_ap, EPS, None, op0=ALU.add))
                    v.wait_ge(vc.sem, vc.val)
                    vc.inc(v.reciprocal(rstd[:, :], rstd[:, :]))
                v_rec4 = vc.val

                @b.scalar
                def _(s):
                    s.wait_ge(vc.sem, v_rec4)
                    sc.inc(s.activation(rstd[:, :], rstd[:, :], AF.Sqrt))
                    s.wait_ge(vc.sem, v_rec4 + 2)
                    for t in range(NTT):
                        if t == 0:
                            s.wait_ge(tn.sem, prev_down_tn)
                        ins = s.activation(xt(xnt, t), xt(x1b, t), AF.Identity,
                                           bias=nmr[:, t:t + 1])
                        sc.inc(ins)
                fin1_done = sc.val
                s_sq4 = fin1_done - NTT

                @b.vector
                def _(v):
                    v.wait_ge(sc.sem, s_sq4)
                    mean_ap = bass.AP(mv, 0, [[2 * NTT, 128], [2, NTT]])
                    vc.inc(v.tensor_tensor(nmr[:, :], mean_ap, rstd[:, :], op=ALU.mult))
                    v.wait_ge(vc.sem, vc.val)
                    vc.inc(v.tensor_scalar(nmr[:, :], nmr[:, :], -1.0, None, op0=ALU.mult))
                assert vc.val == v_rec4 + 2

                @b.vector
                def _(v):
                    for t in range(NTT):
                        v.wait_ge(sc.sem, fin1_done - NTT + t + 1)
                        ins = v.scalar_tensor_tensor(xt(ybuf, t), xt(ybuf, t),
                                                     rstd[:, t:t + 1], xt(xnt, t),
                                                     op0=ALU.mult, op1=ALU.add)
                        vc.inc(ins)
                out_vc = vc.val

                @b.sync
                def _(s):
                    base = out_vc - NTT
                    for t in range(NTT):
                        s.wait_ge(vc.sem, base + t + 1)
                        dst = bass.AP(out.tensor, (q * TQ + t * 128) * D,
                                      [[D, 128], [1, D]])
                        d_out.inc(s.dma_start(dst, xt(ybuf, t)), 16)
            prev_out_dma = d_out.val
            prev_q_attn_usersc = sc.val

        with BLK() as b:
            @b.sync
            def _(s):
                s.wait_ge(d_out.sem, d_out.val)

    nc.finalize()
    return nc


_IDENT = np.eye(128, dtype=ml_dtypes.bfloat16)
_SEG_PERM = np.concatenate([np.arange(0, 512, 2), np.arange(1, 512, 2)])
_FULL_PERM = np.concatenate(
    [b * 2048 + s * 512 + _SEG_PERM for b in range(2) for s in range(4)])


def kernel(L0_attn_x, L0_ffn_gate_weight, L0_ffn_gate_up_weights,
           L0_ffn_down_weights, L0_ffn_shared_gate_up, L0_ffn_shared_down):
    x = np.ascontiguousarray(L0_attn_x, dtype=np.float32)
    wguT = np.ascontiguousarray(
        np.asarray(L0_ffn_gate_up_weights, dtype=np.float32).transpose(2, 0, 1))
    wdnT = np.ascontiguousarray(
        np.asarray(L0_ffn_down_weights, dtype=np.float32).transpose(0, 2, 1))
    sguT = np.ascontiguousarray(
        np.asarray(L0_ffn_shared_gate_up, dtype=np.float32).T)
    sdnT = np.ascontiguousarray(
        np.asarray(L0_ffn_shared_down, dtype=np.float32).T)
    gateT = np.ascontiguousarray(
        np.asarray(L0_ffn_gate_weight, dtype=np.float32).T)

    nc = build_program(NQ)
    in_maps = []
    for c in range(8):
        xc = x[2 * c:2 * c + 2].reshape(TCORE, D)[_FULL_PERM]
        in_maps.append({"x": np.ascontiguousarray(xc), "wguT": wguT,
                        "wdnT": wdnT, "sguT": sguT, "sdnT": sdnT,
                        "gateT": gateT, "ident128": _IDENT})
    res = run_bass_kernel_spmd(nc, in_maps, core_ids=list(range(8)))
    outs = []
    for c in range(8):
        oc = np.asarray(res.results[c]["out"])
        inv = np.empty((TCORE, D), dtype=np.float32)
        inv[_FULL_PERM] = oc
        outs.append(inv.reshape(2, 2048, D))
    return np.concatenate(outs, axis=0)
